# version 01 (final)
# baseline (speedup 1.0000x reference)
import numpy as np
from contextlib import ExitStack
import concourse.bass as bass
import concourse.mybir as mybir
from concourse.bass_utils import run_bass_kernel_spmd

F32 = mybir.dt.float32
BF16 = mybir.dt.bfloat16
AF = mybir.ActivationFunctionType
ALU = mybir.AluOpType
AX = mybir.AxisListType

NCORE = 8
T = 1024
TH = 512
D = 2048
KC = 16
NE = 32
NST = 16
EPS = 1e-5

A_WA, A_WB, A_WC = 0, 2048, 4096
A_HB = 6144
A_WEBC = 14336
A_MBC, A_RBC = 16384, 17408
A_WET = 18432
A_GS = 19456
A_ST = 20480
A_VT = 22528
A_FT = 23552
S_BBC = (16384, 17408)
S_CBC = (18432, 19456)
S_AT0 = 20480
S_BT = (21504, 22528)
S_DTX = 23552
X_WX = 16384
X_VT, X_ST, X_ST2, X_RR = 18432, 18944, 19456, 19968
X_BCR = 20480

SEMS = ("sPE", "sV", "sSC", "sP", "sCC",
        "sS", "sWS", "sLN", "sMB", "sRB",
        "sBB0", "sBB1", "sCB0", "sCB1",
        "sA0", "sA1", "sB0", "sB1", "sC0", "sC1", "sE0", "sE1", "sX0", "sX1")


def prev2(c, n):
    g = c * NST + n - 2
    if g < 0:
        return None
    return divmod(g, NST)


class _MockInstr:
    def then_inc(self, *a, **k):
        return self


class _MockEng:
    def __getattr__(self, name):
        def f(*a, **k):
            return _MockInstr()
        return f


class Ctx:
    def __init__(self, nc, tens, sems):
        self.nc = nc
        self.t = tens
        self.sems = sems
        self.plan = True
        self.ev = {s: {} for s in SEMS}
        self.reset()

    def reset(self):
        self.cnt = {s: 0 for s in SEMS}
        self.free = {}

    def inc(self, instr, sem, key=None, k=1):
        self.cnt[sem] += k
        if key is not None:
            self.ev[sem][key] = self.cnt[sem]
        if not self.plan:
            instr.then_inc(self.sems[sem], k)
        return self.cnt[sem]

    def selfwait(self, eng, sem):
        if not self.plan:
            eng.wait_ge(self.sems[sem], self.cnt[sem])

    def chain(self, eng, ins, sem):
        c = self.inc(ins, sem)
        if not self.plan:
            eng.wait_ge(self.sems[sem], c)
        return ins

    def wait_key(self, eng, ref):
        if ref is None:
            return
        if isinstance(ref, list):
            for r in ref:
                self.wait_key(eng, r)
            return
        sem, key = ref
        v = key if isinstance(key, int) else self.ev[sem].get(key, 0)
        if v > 0 and not self.plan:
            eng.wait_ge(self.sems[sem], v)

    def barrier_waits(self, eng):
        for s in SEMS:
            if self.cnt[s] > 0 and not self.plan:
                eng.wait_ge(self.sems[s], self.cnt[s])


def _sections(ctx, block, **fns):
    order = ("gpsimd", "tensor", "scalar", "vector", "sync")
    for name in order:
        if name in fns:
            if ctx.plan:
                fns[name](_MockEng())
            else:
                getattr(block, name)(fns[name])


def emit_barrier(ctx, block, engines=("tensor", "scalar", "vector", "sync")):
    fns = {e: (lambda eng: ctx.barrier_waits(eng)) for e in engines}
    _sections(ctx, block, **fns)



def emit_init(ctx, block):
    t = ctx.t

    def sync(s):
        ctx.inc(s.dma_start(t["Pj"][:, :], t["alog"][:, :]), "sS", "alog", 16)
        ctx.inc(s.dma_start(t["D_sb"][:, :], t["dvec"][:, :]), "sS", "dvec", 16)
        ctx.inc(s.dma_start(t["ident"][:, :], t["identin"][:, :]), "sS", "ident", 16)
        ctx.inc(s.dma_start(t["rank_sb"][:, :], t["rankv"][:, :]), "sS", "rankv", 16)
        for c in range(KC):
            ins = s.dma_start(t["resid"][:, c * T:(c + 1) * T],
                              t["xt"][c * 128:(c + 1) * 128, :])
            ctx.inc(ins, "sS", ("xt", c), 16)
        ctx.ev["sS"]["init_all"] = ctx.cnt["sS"]

    def gpsimd(g):
        ins = g.dma_start(t["gw_sb"][:, :].rearrange("d (k e) -> d k e", k=16),
                          t["gwt"][:, :, :].rearrange("k d e -> d k e"))
        ctx.inc(ins, "sE0", "gw", 16)

    def scalar(sc):
        ctx.wait_key(sc, ("sS", "init_all"))
        ins = sc.activation(t["A_sb"][:, :], t["Pj"][:, :], AF.Exp)
        ctx.inc(ins, "sSC", "asb_exp")

    def vector(v):
        ctx.wait_key(v, ("sSC", "asb_exp"))
        ins = v.tensor_scalar(t["A_sb"][:, :], t["A_sb"][:, :], -1.0, None, ALU.mult)
        ctx.inc(ins, "sV", "asb_neg")
        ctx.inc(v.memset(t["ones1"][:, :], 1.0), "sV", "ones1")
        ctx.inc(v.memset(t["gones"][:, :], 1.0), "sV", "gones")

    _sections(ctx, block, sync=sync, gpsimd=gpsimd, scalar=scalar, vector=vector)



def emit_ln(ctx, block, which):
    t = ctx.t
    ar = t["arena"]
    sq = [ar[:, A_ST:A_ST + 1024], ar[:, A_ST + 1024:A_ST + 2048]]
    mrow = ar[0:1, A_GS:A_GS + 1024]
    msqrow = ar[0:1, A_ST:A_ST + 1024]
    varrow = ar[0:1, A_ST + 1024:A_ST + 2048]
    m2row = ar[0:1, A_VT + 1024:A_VT + 2048]
    lnrow = ar[0:1, A_GS + 1024 - 0:A_GS + 1024] if False else ar[0:1, A_FT:A_FT + 1024]
    rstdrow = ar[0:1, A_VT:A_VT + 1024]
    ftmp = ar[:, A_FT:A_FT + 1024]
    m_bc = ar[:, A_MBC:A_MBC + 1024]
    r_bc = ar[:, A_RBC:A_RBC + 1024]
    ps = t["ps"]

    def scalar(sc):
        for c in range(KC):
            ctx.wait_key(sc, ("sS", "init_all"))
            ctx.wait_key(sc, ("sPE", ("lnsq_used", which, c - 2)))
            ins = sc.activation(sq[c % 2][:, :], t["resid"][:, c * T:(c + 1) * T],
                                AF.Square)
            ctx.inc(ins, "sSC", ("lnsq", which, c))
        ctx.wait_key(sc, ("sV", ("lnvar", which)))
        ctx.chain(sc, sc.activation(lnrow, varrow, AF.Ln), "sSC")
        ins = sc.activation(rstdrow, lnrow, AF.Exp, scale=-0.5)
        ctx.inc(ins, "sSC", ("lnrstd", which))

    def tensor(te):
        for c in range(KC):
            ctx.wait_key(te, ("sS", "init_all"))
            for th in range(2):
                ins = te.matmul(ps[4 + th][0:1, :], t["ones1"][:, :],
                                t["resid"][:, c * T + th * TH:c * T + (th + 1) * TH],
                                start=(c == 0), stop=(c == KC - 1))
                if c == KC - 1:
                    ctx.inc(ins, "sPE", ("lnsx", which, th))
            ctx.wait_key(te, ("sSC", ("lnsq", which, c)))
            for th in range(2):
                ins = te.matmul(ps[6 + th][0:1, :], t["ones1"][:, :],
                                sq[c % 2][:, th * TH:(th + 1) * TH],
                                start=(c == 0), stop=(c == KC - 1))
            ctx.inc(ins, "sPE", ("lnsq_used", which, c))
            if c == KC - 1:
                ctx.ev["sPE"][("lnsxx", which)] = ctx.cnt["sPE"]

    def vector(v):
        ctx.wait_key(v, ("sPE", ("lnsx", which, 0)))
        ctx.wait_key(v, ("sPE", ("lnsx", which, 1)))
        ctx.wait_key(v, ("sPE", ("lnsxx", which)))
        for th in range(2):
            v.tensor_scalar(mrow[:, th * TH:(th + 1) * TH], ps[4 + th][0:1, :],
                            1.0 / D, None, ALU.mult)
            ins = v.tensor_scalar(msqrow[:, th * TH:(th + 1) * TH], ps[6 + th][0:1, :],
                                  1.0 / D, None, ALU.mult)
        ctx.chain(v, ins, "sV")
        ctx.chain(v, v.tensor_tensor(m2row, mrow, mrow, ALU.mult), "sV")
        ctx.chain(v, v.tensor_tensor(varrow, msqrow, m2row, ALU.subtract), "sV")
        ins = v.tensor_scalar(varrow, varrow, EPS, None, ALU.add)
        ctx.inc(ins, "sV", ("lnvar", which))
        for c in range(KC):
            ctx.wait_key(v, ("sMB", ("lnbcm", which)))
            ctx.wait_key(v, ("sRB", ("lnbc", which)))
            ctx.chain(v, v.tensor_tensor(ftmp, t["resid"][:, c * T:(c + 1) * T],
                                         m_bc, ALU.subtract), "sV")
            ins = v.tensor_tensor(t["xf"][:, c * T:(c + 1) * T], ftmp, r_bc, ALU.mult)
            ctx.inc(ins, "sV", ("xf", which, c))
            ctx.selfwait(v, "sV")

    def sync(s):
        ctx.wait_key(s, ("sV", ("lnvar", which)))
        ctx.wait_key(s, ("sSC", ("lnrstd", which)))
        ctx.inc(s.dma_start(t["lnscr"][0:1, :], mrow), "sLN", None, 16)
        c0 = ctx.inc(s.dma_start(t["lnscr"][1:2, :], rstdrow), "sLN", ("lnscr", which), 16)
        if not ctx.plan:
            s.wait_ge(ctx.sems["sLN"], c0)
        src_m = bass.AP(t["lnscr"], 0, [[0, 128], [1, 1024]])
        src_r = bass.AP(t["lnscr"], 1024, [[0, 128], [1, 1024]])
        ctx.inc(s.dma_start(m_bc, src_m), "sMB", ("lnbcm", which), 16)
        ctx.inc(s.dma_start(r_bc, src_r), "sRB", ("lnbc", which), 16)

    _sections(ctx, block, tensor=tensor, scalar=scalar, vector=vector, sync=sync)



def emit_gate(ctx, block, call):
    t = ctx.t
    ar = t["arena"]
    ps = t["ps"]
    gsc = ar[:, A_GS:A_GS + 32]
    gex = ar[:, A_GS + 32:A_GS + 64]
    geq = ar[:, A_GS + 64:A_GS + 96]
    gmsk = ar[:, A_GS + 96:A_GS + 128]
    ggt = ar[:, A_GS + 128:A_GS + 160]
    gpfx = ar[:, A_GS + 160:A_GS + 192]
    gwf = ar[:, A_GS + 192:A_GS + 448]
    gc = t["gcols"]
    weT = ar[0:32, A_WET:A_WET + 1024]

    def col(i):
        return gc[:, i:i + 1]

    def tensor(te):
        ctx.wait_key(te, ("sE0", "gw"))
        for j in range(8):
            pg = ps[4 + 2 * (j % 2)]
            ctx.wait_key(te, ("sV", ("gevac", call, j - 2)))
            for k in range(KC):
                ins = te.matmul(pg[:, 0:32],
                                t["xf"][:, k * T + j * 128:k * T + (j + 1) * 128],
                                t["gw_sb"][:, k * 32:(k + 1) * 32],
                                start=(k == 0), stop=(k == KC - 1))
            ctx.inc(ins, "sPE", ("gmm", call, j))
            ctx.wait_key(te, ("sV", ("gwf", call, j)))
            pt = ps[5 + 2 * (j % 2)]
            ins = te.transpose(pt[0:32, 0:128], gwf[:, j * 32:(j + 1) * 32],
                               t["ident"][:, :])
            ctx.inc(ins, "sPE", ("gtr", call, j))

    def vector(v):
        for j in range(8):
            pg = ps[4 + 2 * (j % 2)]
            ctx.wait_key(v, ("sPE", ("gmm", call, j)))
            ch = lambda ins: ctx.chain(v, ins, "sV")
            ch(v.reduce_max(col(0), pg[:, 0:32], AX.X))
            ins = v.tensor_scalar(gsc, pg[:, 0:32], col(0), None, ALU.subtract)
            ctx.inc(ins, "sV", ("gsub", call, j))
            ctx.wait_key(v, ("sSC", ("gexp", call, j)))
            ch(v.reduce_sum(col(1), gex, AX.X))
            ch(v.reciprocal(col(2), col(1)))
            ch(v.tensor_scalar(gsc, gex, col(2), None, ALU.mult))
            ch(v.reduce_max(col(3), gsc, AX.X))
            ch(v.tensor_scalar(geq, gsc, col(3), None, ALU.is_equal))
            ch(v.reduce_sum(col(4), geq, AX.X))
            ch(v.scalar_tensor_tensor(gmsk, geq, -100.0, gsc, ALU.mult, ALU.add))
            ch(v.reduce_max(col(5), gmsk, AX.X))
            ch(v.tensor_scalar(geq, gmsk, col(5), None, ALU.is_equal))
            ch(v.reduce_sum(col(6), geq, AX.X))
            ch(v.tensor_tensor(col(6), col(6), col(4), ALU.add))
            ch(v.scalar_tensor_tensor(gmsk, geq, -100.0, gmsk, ALU.mult, ALU.add))
            ch(v.reduce_max(col(7), gmsk, AX.X))
            ch(v.tensor_scalar(geq, gmsk, col(7), None, ALU.is_equal))
            ch(v.reduce_sum(col(8), geq, AX.X))
            ch(v.tensor_tensor(col(8), col(8), col(6), ALU.add))
            ch(v.scalar_tensor_tensor(gmsk, geq, -100.0, gmsk, ALU.mult, ALU.add))
            ch(v.reduce_max(col(9), gmsk, AX.X))
            ch(v.tensor_copy(col(10), col(9)))
            for cum_i, val_i in ((8, 7), (6, 5), (4, 3)):
                ch(v.tensor_scalar(col(11), col(cum_i), 4.0, None, ALU.is_ge))
                ch(v.tensor_tensor(col(12), col(val_i), col(10), ALU.subtract))
                ch(v.scalar_tensor_tensor(col(10), col(12), col(11), col(10),
                                          ALU.mult, ALU.add))
            ch(v.tensor_scalar(ggt, gsc, col(10), None, ALU.is_gt))
            ch(v.reduce_sum(col(11), ggt, AX.X))
            ch(v.tensor_scalar(col(11), col(11), -1.0, 4.0, ALU.mult, ALU.add))
            ch(v.tensor_scalar(geq, gsc, col(10), None, ALU.is_equal))
            ch(v.tensor_tensor_scan(gpfx, t["gones"][:, :], geq, 0.0, ALU.mult, ALU.add))
            ch(v.tensor_tensor(gpfx, gpfx, geq, ALU.subtract))
            ch(v.tensor_scalar(gpfx, gpfx, col(11), None, ALU.is_lt))
            ch(v.tensor_tensor(geq, geq, gpfx, ALU.mult))
            ch(v.tensor_tensor(ggt, ggt, geq, ALU.add))
            ins = v.scalar_tensor_tensor(gwf[:, j * 32:(j + 1) * 32], gsc, 0.5, ggt,
                                         ALU.mult, ALU.mult)
            ctx.inc(ins, "sV", ("gwf", call, j))
            ctx.wait_key(v, ("sPE", ("gtr", call, j)))
            pt = ps[5 + 2 * (j % 2)]
            ins = v.tensor_copy(weT[:, j * 128:(j + 1) * 128], pt[0:32, 0:128])
            ctx.inc(ins, "sV", ("gevac", call, j))

    def scalar(sc):
        for j in range(8):
            ctx.wait_key(sc, ("sV", ("gsub", call, j)))
            ins = sc.activation(gex, gsc, AF.Exp)
            ctx.inc(ins, "sSC", ("gexp", call, j))

    def sync(s):
        ctx.wait_key(s, ("sV", ("gevac", call, 7)))
        ctx.wait_key(s, ("sE0", ("webc_last", call - 1, 0)))
        ctx.wait_key(s, ("sE1", ("webc_last", call - 1, 1)))
        ins = s.dma_start(t["wescr"][:, :], weT)
        ctx.inc(ins, "sWS", ("wescr", call), 16)

    _sections(ctx, block, tensor=tensor, scalar=scalar, vector=vector, sync=sync)



def emit_ffn(ctx, block, call, n_experts=NE, do_shared=True, pre_wait=None,
             pre_wait_sc=None):
    t = ctx.t
    ar = t["arena"]
    ps = t["ps"]
    wAv = ar[:, A_WA:A_WA + 2048].bitcast(BF16)
    wBv = ar[:, A_WB:A_WB + 2048].bitcast(BF16)
    wCv = ar[:, A_WC:A_WC + 2048].bitcast(BF16)
    hv = ar[:, A_HB:A_HB + 8192].bitcast(BF16)
    webc = ar[:, A_WEBC:A_WEBC + 2048]
    stmp = [ar[:, A_ST + i * TH:A_ST + (i + 1) * TH] for i in range(4)]
    vtmp = [ar[:, A_VT + i * TH:A_VT + (i + 1) * TH] for i in range(2)]

    jobs = ([("sh", None)] if do_shared else []) + [("e", e) for e in range(n_experts)]
    jm = lambda job: 16 if job[0] == "sh" else 8
    jk2 = lambda job: 16 if job[0] == "sh" else 8

    def wtile(view, m, k):
        return view[:, (m % 2) * 2048 + k * 128:(m % 2) * 2048 + (k + 1) * 128]

    def wc_tile(job, c, k):
        if job[0] == "sh":
            return wCv[:, (c % 2) * 2048 + k * 128:(c % 2) * 2048 + (k + 1) * 128]
        base = ((c // 2) % 2) * 2048 + (c % 2) * 1024
        return wCv[:, base + k * 128:base + (k + 1) * 128]

    def h_tile(m, th):
        return hv[:, m * T + th * TH:m * T + (th + 1) * TH]

    def p1_banks(m):
        return (ps[2 * (m % 2)], ps[2 * (m % 2) + 1])

    def py_bank(c, th):
        return ps[4 + 2 * (c % 2) + th]

    def phase1_order(M):
        seq = []
        for mp in range(0, M, 2):
            seq += [("mm1", mp), ("mm1", mp + 1), ("mm3", mp), ("mm3", mp + 1)]
        return seq

    def gpsimd(g):
        first = True
        for ji, job in enumerate(jobs):
            M = jm(job)
            if job[0] == "e":
                e = job[1]
                par = e % 2
                sem = "sE%d" % par
                ctx.wait_key(g, ("sWS", ("wescr", call)))
                ctx.wait_key(g, ctx.free.get(("webc", par)))
                src = bass.AP(t["wescr"], e * T, [[0, 128], [1, T]])
                ins = g.dma_start(webc[:, par * T:(par + 1) * T], src)
                ctx.inc(ins, sem, ("dwebc", call, e), 16)
                ctx.free[("webc", par)] = ("sV", ("h", call, ji, M - 1, 1))
                if e >= n_experts - 2:
                    ctx.ev[sem][("webc_last", call, par)] = ctx.cnt[sem]
            for m in range(M):
                par = m % 2
                for (nm, srcn, dstv, lay, semf) in (
                        ("dwA", "w1", wAv, "mm1", "sA"), ("dwB", "w3", wBv, "mm3", "sB")):
                    sem = semf + str(par)
                    if first:
                        ctx.wait_key(g, pre_wait)
                    ctx.wait_key(g, ctx.free.get((nm, par)))
                    if job[0] == "sh":
                        srcap = t["s" + srcn + "t"][m].rearrange("k d i -> d k i")
                    else:
                        srcap = t[srcn + "t"][job[1], m].rearrange("k d i -> d k i")
                    dst = dstv[:, par * 2048:(par + 1) * 2048].rearrange(
                        "d (k i) -> d k i", k=16)
                    ins = g.dma_start(dst, srcap)
                    ctx.inc(ins, sem, (nm, call, ji, m), 16)
                    ctx.free[(nm, par)] = ("sPE", (lay, call, ji, m, 1))
                    first = False
            if job[0] == "sh":
                for c in range(16):
                    par = c % 2
                    ctx.wait_key(g, pre_wait if first else None)
                    first = False
                    ctx.wait_key(g, ctx.free.get(("dwC", par)))
                    srcap = t["sw2t"][c].rearrange("k i d -> i k d")
                    dst = wCv[:, par * 2048:(par + 1) * 2048].rearrange(
                        "i (k d) -> i k d", k=16)
                    ins = g.dma_start(dst, srcap)
                    ctx.inc(ins, "sC%d" % par, ("dwC", call, ji, c), 16)
                    ctx.free[("dwC", par)] = ("sPE", ("ymm", call, ji, c, 1))
            else:
                for cg in range(8):
                    par = cg % 2
                    ctx.wait_key(g, pre_wait if first else None)
                    first = False
                    ctx.wait_key(g, ctx.free.get(("dwC", par)))
                    srcap = t["w2t"][job[1], 2 * cg:2 * cg + 2].rearrange(
                        "c k i d -> i (c k) d")
                    dst = wCv[:, par * 2048:(par + 1) * 2048].rearrange(
                        "i (ck d) -> i ck d", ck=16)
                    ins = g.dma_start(dst, srcap)
                    ctx.inc(ins, "sC%d" % par, ("dwC", call, ji, cg), 16)
                    ctx.free[("dwC", par)] = ("sPE", ("ymm", call, ji, 2 * cg + 1, 1))

    def tensor(te):
        for ji, job in enumerate(jobs):
            M, K2 = jm(job), jk2(job)
            for (lay, m) in phase1_order(M):
                par = m % 2
                pb = p1_banks(m)
                dname = "dwA" if lay == "mm1" else "dwB"
                view = wAv if lay == "mm1" else wBv
                dsem = ("sA" if lay == "mm1" else "sB") + str(par)
                ctx.wait_key(te, (dsem, (dname, call, ji, m)))
                if lay == "mm1":
                    ctx.wait_key(te, ctx.free.get(("p1", par)))
                else:
                    ctx.wait_key(te, ("sSC", ("silu", call, ji, m, 0)))
                    ctx.wait_key(te, ("sSC", ("silu", call, ji, m, 1)))
                for th in range(2):
                    for k in range(KC):
                        ins = te.matmul(
                            pb[th][:, :], wtile(view, m, k),
                            t["xf"][:, k * T + th * TH:k * T + (th + 1) * TH],
                            start=(k == 0), stop=(k == KC - 1))
                    ctx.inc(ins, "sPE", (lay, call, ji, m, th))
                if lay == "mm3":
                    ctx.free[("p1", par)] = ("sV", ("h", call, ji, m, 1))
            for c in range(16):
                if job[0] == "sh":
                    dkey, dsem = ("dwC", call, ji, c), "sC%d" % (c % 2)
                else:
                    dkey, dsem = ("dwC", call, ji, c // 2), "sC%d" % ((c // 2) % 2)
                ctx.wait_key(te, (dsem, dkey))
                ctx.wait_key(te, ctx.free.get(("py", c % 2)))
                if c == 0:
                    ctx.wait_key(te, ("sV", ("h", call, ji, M - 1, 1)))
                for th in range(2):
                    pb = py_bank(c, th)
                    for k in range(K2):
                        ins = te.matmul(
                            pb[:, :], wc_tile(job, c, k), h_tile(k, th),
                            start=(k == 0), stop=(k == K2 - 1))
                    ctx.inc(ins, "sPE", ("ymm", call, ji, c, th))
                ctx.free[("py", c % 2)] = ("sV", ("yev", call, ji, c, 1))

    def scalar(sc):
        ctx.wait_key(sc, pre_wait_sc)
        for ji, job in enumerate(jobs):
            for m in range(jm(job)):
                for th in range(2):
                    slot = (m % 2) * 2 + th
                    ctx.wait_key(sc, ("sPE", ("mm1", call, ji, m, th)))
                    ctx.wait_key(sc, ctx.free.get(("stmp", slot)))
                    ins = sc.activation(stmp[slot], p1_banks(m)[th][:, :], AF.Silu)
                    ctx.inc(ins, "sSC", ("silu", call, ji, m, th))
                    ctx.free[("stmp", slot)] = ("sV", ("h", call, ji, m, th))

    def vector(v):
        for ji, job in enumerate(jobs):
            M = jm(job)
            for m in range(M):
                pb = p1_banks(m)
                for th in range(2):
                    slot = (m % 2) * 2 + th
                    ctx.wait_key(v, ("sPE", ("mm3", call, ji, m, th)))
                    if job[0] == "e":
                        ctx.wait_key(v, ("sE%d" % (job[1] % 2),
                                         ("dwebc", call, job[1])))
                        ins = v.tensor_tensor(vtmp[th], pb[th][:, :],
                                              webc[:, (job[1] % 2) * T + th * TH:
                                                   (job[1] % 2) * T + (th + 1) * TH],
                                              ALU.mult)
                    else:
                        ins = v.tensor_scalar(vtmp[th], pb[th][:, :], 0.5, None, ALU.mult)
                    ctx.chain(v, ins, "sV")
                    ctx.wait_key(v, ("sSC", ("silu", call, ji, m, th)))
                    if ji == 0:
                        ctx.wait_key(v, ctx.free.get(("hbuf",)))
                    else:
                        ctx.wait_key(v, ("sPE", ("ymm", call, ji - 1, 15, 1)))
                    ins = v.tensor_tensor(h_tile(m, th), stmp[slot], vtmp[th], ALU.mult)
                    ctx.inc(ins, "sV", ("h", call, ji, m, th))
            for c in range(16):
                for th in range(2):
                    ctx.wait_key(v, ("sPE", ("ymm", call, ji, c, th)))
                    sl = t["resid"][:, c * T + th * TH:c * T + (th + 1) * TH]
                    ins = v.tensor_tensor(sl, sl, py_bank(c, th)[:, :], ALU.add)
                    ctx.inc(ins, "sV", ("yev", call, ji, c, th))
        ctx.free[("hbuf",)] = ("sPE", ("ymm", call, len(jobs) - 1, 15, 1))

    _sections(ctx, block, gpsimd=gpsimd, tensor=tensor, scalar=scalar, vector=vector)



def emit_xproj(ctx, block, raw=False, bank_base=0):
    t = ctx.t
    ar = t["arena"]
    ps = t["ps"]
    wxv = ar[:, X_WX:X_WX + 2048].bitcast(BF16)
    vt = ar[:, X_VT:X_VT + TH]
    st = ar[:, X_ST:X_ST + TH]
    st2 = ar[:, X_ST2:X_ST2 + TH]
    rr = ar[:, X_RR:X_RR + TH]
    bcr = ar[0:32, X_BCR:X_BCR + 1024]

    def gpsimd(g):
        for m in range(17):
            par = m % 2
            if m < 2:
                ctx.wait_key(g, ("sV", ("xf", 1, KC - 1)))
            ctx.wait_key(g, ctx.free.get(("wx", par)))
            src = t["wxt"][m].rearrange("k d e -> d k e")
            dst = wxv[:, par * 2048:(par + 1) * 2048].rearrange("d (k e) -> d k e", k=16)
            ins = g.dma_start(dst, src)
            ctx.inc(ins, "sX%d" % par, ("dwx", m), 16)
            if m < 16:
                ctx.free[("wx", par)] = ("sV", ("xdt", m, 1))
            else:
                ctx.free[("wx", par)] = ("sV", ("xbc", 1))

    def tensor(te):
        for m in range(17):
            par = m % 2
            ctx.wait_key(te, ("sX%d" % par, ("dwx", m)))
            ctx.wait_key(te, ctx.free.get(("xps", par)))
            for th in range(2):
                pb = ps[bank_base + 2 * par + th]
                for k in range(KC):
                    ins = te.matmul(pb[:, :],
                                    wxv[:, par * 2048 + k * 128:par * 2048 + (k + 1) * 128],
                                    t["xf"][:, k * T + th * TH:k * T + (th + 1) * TH],
                                    start=(k == 0), stop=(k == KC - 1))
                ctx.inc(ins, "sPE", ("xmm", m, th))
            if m < 16:
                ctx.free[("xps", par)] = ("sV", ("xdt", m, 1))
            else:
                ctx.free[("xps", par)] = ("sV", ("xbc", 1))

    def scalar(sc):
        if raw:
            return
        ctx.wait_key(sc, ("sWS", ("wescr", 0)))
        for m in range(16):
            for th in range(2):
                pb = ps[bank_base + 2 * (m % 2) + th]
                ctx.wait_key(sc, ("sPE", ("xmm", m, th)))
                ctx.wait_key(sc, ("sV", ("xdt", m, th - 1) if th == 1 else
                                 ("xdt", m - 1, 1)))
                ctx.chain(sc, sc.activation(vt, pb[:, :], AF.Abs), "sSC")
                ctx.chain(sc, sc.activation(st, vt, AF.Exp, scale=-1.0), "sSC")
                sc.activation(st2, st, AF.Ln, bias=1.0)
                ins = sc.activation(rr, pb[:, :], AF.Relu)
                ctx.inc(ins, "sSC", ("xsp", m, th))

    def vector(v):
        for m in range(17):
            par = m % 2
            for th in range(2):
                pb = ps[bank_base + 2 * par + th]
                ctx.wait_key(v, ("sPE", ("xmm", m, th)))
                if m < 16:
                    if raw:
                        ins = v.tensor_copy(
                            ar[:, m * T + th * TH:m * T + (th + 1) * TH], pb[:, :])
                        ctx.inc(ins, "sV", ("xdt", m, th))
                        continue
                    ctx.wait_key(v, ("sSC", ("xsp", m, th)))
                    ins = v.tensor_tensor(
                        ar[:, m * T + th * TH:m * T + (th + 1) * TH], st2, rr, ALU.add)
                    ctx.inc(ins, "sV", ("xdt", m, th))
                else:
                    ins = v.tensor_copy(bcr[:, th * TH:(th + 1) * TH], pb[0:32, :])
                    ctx.inc(ins, "sV", ("xbc", th))

    def sync(s):
        ctx.wait_key(s, ("sV", ("xbc", 0)))
        ctx.wait_key(s, ("sV", ("xbc", 1)))
        ins = s.dma_start(t["rowscr"][:, :], bcr)
        ctx.inc(ins, "sWS", ("rowscr",), 16)

    _sections(ctx, block, gpsimd=gpsimd, tensor=tensor, scalar=scalar, vector=vector,
              sync=sync)



def _bcast(src_t, row, width=1024):
    return bass.AP(src_t, row * width, [[0, 128], [1, width]])


def emit_scan(ctx, block):
    t = ctx.t
    ar = t["arena"]
    dt_c = lambda c: ar[:, c * T:(c + 1) * T]
    bbc = [ar[:, S_BBC[0]:S_BBC[0] + T], ar[:, S_BBC[1]:S_BBC[1] + T]]
    cbc = [ar[:, S_CBC[0]:S_CBC[0] + T], ar[:, S_CBC[1]:S_CBC[1] + T]]
    at = [ar[:, S_AT0:S_AT0 + T], t["axt"][:, :]]
    bt = [ar[:, S_BT[0]:S_BT[0] + T], ar[:, S_BT[1]:S_BT[1] + T]]
    dtx = ar[:, S_DTX:S_DTX + T]

    def sync(s):
        ctx.wait_key(s, ("sWS", ("rowscr",)))
        for c in range(KC):
            for n in range(NST):
                p2 = prev2(c, n)
                if p2 is not None:
                    ctx.wait_key(s, ("sV", ("sb",) + p2))
                ctx.inc(s.dma_start(bbc[n % 2], _bcast(t["rowscr"], n)),
                        "sBB%d" % (n % 2), ("sbb", c, n), 16)
                if p2 is not None:
                    ctx.wait_key(s, ("sV", ("st",) + p2))
                ins = s.dma_start(cbc[n % 2], _bcast(t["rowscr"], 16 + n))
                ctx.inc(ins, "sCB%d" % (n % 2), ("sbc", c, n), 16)

    def scalar(sc):
        ctx.wait_key(sc, ("sWS", ("rowscr",)))
        for c in range(KC):
            ctx.wait_key(sc, ("sV", ("xdt", c, 1)))
            for n in range(NST):
                g = c * NST + n
                par = g % 2
                p2 = prev2(c, n)
                if p2 is not None:
                    ctx.wait_key(sc, ("sV", ("st",) + p2))
                    ctx.wait_key(sc, ("sP", ("ss",) + p2))
                ins = sc.activation(at[par], dt_c(c), AF.Exp,
                                    scale=t["A_sb"][:, c * NST + n:c * NST + n + 1])
                ctx.inc(ins, "sSC", ("sa", c, n))
            ctx.wait_key(sc, ("sV", ("sumdt", c)))
            ins = sc.activation(t["P_sb"][:, c * NST:(c + 1) * NST],
                                t["A_sb"][:, c * NST:(c + 1) * NST],
                                AF.Exp, scale=t["scol"][:, 0:1])
            ctx.inc(ins, "sSC", ("sp", c))

    def vector(v):
        for c in range(KC):
            sl = t["resid"][:, c * T:(c + 1) * T]
            v.scalar_tensor_tensor(sl, t["xf"][:, c * T:(c + 1) * T],
                                   t["D_sb"][:, c:c + 1], sl, ALU.mult, ALU.add)
            ctx.wait_key(v, ("sSC", ("sp", c - 1)))
            ctx.chain(v, v.tensor_tensor(dtx, dt_c(c), t["xf"][:, c * T:(c + 1) * T],
                                         ALU.mult), "sV")
            ins = v.reduce_sum(t["scol"][:, 0:1], dt_c(c), AX.X)
            ctx.inc(ins, "sV", ("sumdt", c))
            for n in range(NST):
                g = c * NST + n
                par = g % 2
                p2 = prev2(c, n)
                ctx.wait_key(v, ("sBB%d" % (n % 2), ("sbb", c, n)))
                ctx.wait_key(v, ("sCB%d" % (n % 2), ("sbc", c, n)))
                if p2 is not None:
                    ctx.wait_key(v, ("sP", ("sr",) + p2))
                ctx.chain(v, v.tensor_tensor(bt[n % 2], dtx, bbc[n % 2], ALU.mult), "sV")
                ctx.ev["sV"][("sb", c, n)] = ctx.cnt["sV"]
                ctx.wait_key(v, ("sSC", ("sa", c, n)))
                if p2 is not None:
                    ctx.wait_key(v, ("sP", ("ss",) + p2))
                ctx.chain(v, v.tensor_tensor_scan(at[par], at[par], bt[n % 2], 0.0,
                                                  ALU.mult, ALU.add), "sV")
                ins = v.tensor_tensor(bt[n % 2], at[par], cbc[n % 2], ALU.mult)
                ctx.inc(ins, "sV", ("st", c, n))

    def gpsimd(g):
        for c in range(KC):
            sl = t["resid"][:, c * T:(c + 1) * T]
            for n in range(NST):
                par = (c * NST + n) % 2
                ctx.wait_key(g, ("sV", ("st", c, n)))
                ins = g.tensor_tensor(sl, sl, bt[n % 2], ALU.add)
                ctx.inc(ins, "sP", ("sr", c, n))
                ctx.selfwait(g, "sP")
                ins = g.tensor_copy(t["S_sb"][:, c * NST + n:c * NST + n + 1],
                                    at[par][:, T - 1:T])
                ctx.inc(ins, "sP", ("ss", c, n))
                ctx.selfwait(g, "sP")

    _sections(ctx, block, sync=sync, scalar=scalar, vector=vector, gpsimd=gpsimd)



def emit_exchange(ctx, block):
    t = ctx.t

    def sync(s):
        ctx.wait_key(s, ("sP", ("ss", KC - 1, NST - 1)))
        ctx.wait_key(s, ("sP", ("sr", KC - 1, NST - 1)))
        ctx.wait_key(s, ("sSC", ("sp", KC - 1)))
        ctx.inc(s.dma_start(t["ccin"][0:128, :], t["S_sb"][:, :]),
                "sBB0", ("ccinS",), 16)
        ctx.inc(s.dma_start(t["ccin"][128:256, :], t["P_sb"][:, :]),
                "sBB1", ("ccinP",), 16)
        for j in range(8):
            ctx.wait_key(s, ("sCC", ("cc",)))
            ctx.wait_key(s, ("sV", ("hstep", j - 1)))
            ctx.inc(s.dma_start(t["Sj"][:, :], t["ccout"][256 * j:256 * j + 128, :]),
                    "sCB0", ("ccpairS", j), 16)
            ins = s.dma_start(t["Pj"][:, :], t["ccout"][256 * j + 128:256 * j + 256, :])
            ctx.inc(ins, "sCB1", ("ccpair", j), 16)

    def gpsimd(g):
        ctx.wait_key(g, ("sBB0", ("ccinS",)))
        ctx.wait_key(g, ("sBB1", ("ccinP",)))
        ins = g.collective_compute(
            "AllGather", ALU.bypass, replica_groups=[list(range(NCORE))],
            ins=[t["ccin"][:, :]], outs=[t["ccout"][:, :]])
        ctx.inc(ins, "sCC", ("cc",), 1)

    def vector(v):
        ctx.chain(v, v.memset(t["Hcur"][:, :], 0.0), "sV")
        ctx.chain(v, v.memset(t["Hsel"][:, :], 0.0), "sV")
        for j in range(8):
            ctx.wait_key(v, ("sCB0", ("ccpairS", j)))
            ctx.wait_key(v, ("sCB1", ("ccpair", j)))
            ctx.chain(v, v.scalar_tensor_tensor(
                t["Hsel"][:, :], t["Hcur"][:, :], t["rank_sb"][:, j:j + 1],
                t["Hsel"][:, :], ALU.mult, ALU.add), "sV")
            ctx.chain(v, v.tensor_tensor(t["Hcur"][:, :], t["Hcur"][:, :],
                                         t["Pj"][:, :], ALU.mult), "sV")
            ins = v.tensor_tensor(t["Hcur"][:, :], t["Hcur"][:, :], t["Sj"][:, :], ALU.add)
            ctx.inc(ins, "sV", ("hstep", j))
            ctx.selfwait(v, "sV")

    _sections(ctx, block, sync=sync, gpsimd=gpsimd, vector=vector)


def emit_corr(ctx, block):
    t = ctx.t
    ar = t["arena"]
    dt_c = lambda c: ar[:, c * T:(c + 1) * T]
    zeros = ar[:, S_BBC[0]:S_BBC[0] + T]
    cbc = [ar[:, S_CBC[0]:S_CBC[0] + T], ar[:, S_CBC[1]:S_CBC[1] + T]]
    at = [ar[:, S_AT0:S_AT0 + T], t["axt"][:, :]]
    bt = [ar[:, S_BT[0]:S_BT[0] + T], ar[:, S_BT[1]:S_BT[1] + T]]
    cum = ar[:, S_DTX:S_DTX + T]
    last_scan = [("sV", ("st", KC - 1, NST - 1)), ("sP", ("ss", KC - 1, NST - 1)),
                 ("sP", ("sr", KC - 1, NST - 1))]

    def sync(s):
        ctx.wait_key(s, last_scan)
        ctx.wait_key(s, ("sV", ("hstep", 7)))
        for c in range(KC):
            for n in range(NST):
                p2 = prev2(c, n)
                if p2 is not None:
                    ctx.wait_key(s, ("sV", ("ct",) + p2))
                ins = s.dma_start(cbc[n % 2], _bcast(t["rowscr"], 16 + n))
                ctx.inc(ins, "sCB%d" % (n % 2), ("cbc2", c, n), 16)

    def vector(v):
        ctx.wait_key(v, last_scan)
        ctx.chain(v, v.memset(zeros, 0.0), "sV")
        for c in range(KC):
            ctx.wait_key(v, ("sSC", ("ca", c - 1, NST - 1)))
            ins = v.tensor_tensor_scan(cum, zeros, dt_c(c), 0.0, ALU.add, ALU.add)
            ctx.inc(ins, "sV", ("cum", c))
            for n in range(NST):
                g = c * NST + n
                par = g % 2
                p2 = prev2(c, n)
                ctx.wait_key(v, ("sSC", ("ca", c, n)))
                ctx.wait_key(v, ("sCB%d" % (n % 2), ("cbc2", c, n)))
                if p2 is not None:
                    ctx.wait_key(v, ("sP", ("cr",) + p2))
                ins = v.scalar_tensor_tensor(
                    bt[n % 2], at[par], t["Hsel"][:, c * NST + n:c * NST + n + 1],
                    cbc[n % 2], ALU.mult, ALU.mult)
                ctx.inc(ins, "sV", ("ct", c, n))

    def scalar(sc):
        ctx.wait_key(sc, last_scan)
        for c in range(KC):
            ctx.wait_key(sc, ("sV", ("cum", c)))
            for n in range(NST):
                par = (c * NST + n) % 2
                p2 = prev2(c, n)
                if p2 is not None:
                    ctx.wait_key(sc, ("sV", ("ct",) + p2))
                ins = sc.activation(at[par], cum, AF.Exp,
                                    scale=t["A_sb"][:, c * NST + n:c * NST + n + 1])
                ctx.inc(ins, "sSC", ("ca", c, n))

    def gpsimd(g):
        for c in range(KC):
            sl = t["resid"][:, c * T:(c + 1) * T]
            for n in range(NST):
                ctx.wait_key(g, ("sV", ("ct", c, n)))
                ins = g.tensor_tensor(sl, sl, bt[n % 2], ALU.add)
                ctx.chain(g, ins, "sP")
                ctx.ev["sP"][("cr", c, n)] = ctx.cnt["sP"]

    _sections(ctx, block, sync=sync, vector=vector, scalar=scalar, gpsimd=gpsimd)



def emit_store(ctx, block):
    t = ctx.t

    def sync(s):
        ctx.barrier_waits(s)
        for c in range(KC):
            ins = s.dma_start(t["out"][c * 128:(c + 1) * 128, :],
                              t["resid"][:, c * T:(c + 1) * T])
            ctx.inc(ins, "sS", ("store", c), 16)
        if not ctx.plan:
            s.wait_ge(ctx.sems["sS"], ctx.cnt["sS"])
            if ctx.cnt["sWS"] > 0:
                s.wait_ge(ctx.sems["sWS"], ctx.cnt["sWS"])

    _sections(ctx, block, sync=sync)


def emit_dump(ctx, block, what):
    t = ctx.t

    def gpsimd(g):
        ctx.barrier_waits(g)
        for c in range(KC):
            if what == "xf":
                src = t["xf"][:, c * T:(c + 1) * T]
            elif what == "dt":
                src = t["arena"][:, c * T:(c + 1) * T]
            elif what == "arena2":
                if c >= 8:
                    continue
                src = t["arena"][:, 16384 + c * T:16384 + (c + 1) * T]
            elif what == "exch":
                if c >= 4:
                    continue
                src = [t["S_sb"], t["P_sb"], t["Hsel"], t["Hcur"]][c][:, :]
                ins = g.dma_start(t["dbg"][c * 128:(c + 1) * 128, 0:256], src)
                ctx.inc(ins, "sWS", ("dump", c), 16)
                continue
            else:
                src = t["resid"][:, c * T:(c + 1) * T]
            ins = g.dma_start(t["dbg"][c * 128:(c + 1) * 128, :], src)
            ctx.inc(ins, "sWS", ("dump", c), 16)

    _sections(ctx, block, gpsimd=gpsimd)



def build_program(do_ffn1=True, do_mamba=True, do_ffn2=True, n_experts=NE,
                  do_shared=True, dump=None, do_ln1=False, do_gate1=False,
                  mamba_stage="corr"):
    nc = bass.Bass("TRN2", target_bir_lowering=False)
    t = {}
    t["xt"] = nc.dram_tensor("xt", [D, T], F32, kind="ExternalInput")
    any_ffn = do_ffn1 or do_ffn2
    if any_ffn and n_experts > 0:
        t["w1t"] = nc.dram_tensor("w1t", [NE, 8, 16, 128, 128], F32, kind="ExternalInput")
        t["w3t"] = nc.dram_tensor("w3t", [NE, 8, 16, 128, 128], F32, kind="ExternalInput")
        t["w2t"] = nc.dram_tensor("w2t", [NE, 16, 8, 128, 128], F32, kind="ExternalInput")
    if any_ffn and do_shared:
        t["sw1t"] = nc.dram_tensor("sw1t", [16, 16, 128, 128], F32, kind="ExternalInput")
        t["sw3t"] = nc.dram_tensor("sw3t", [16, 16, 128, 128], F32, kind="ExternalInput")
        t["sw2t"] = nc.dram_tensor("sw2t", [16, 16, 128, 128], F32, kind="ExternalInput")
    if do_mamba:
        t["wxt"] = nc.dram_tensor("wxt", [17, 16, 128, 128], F32, kind="ExternalInput")
    t["gwt"] = nc.dram_tensor("gwt", [16, 128, 32], F32, kind="ExternalInput")
    t["alog"] = nc.dram_tensor("alog", [128, 256], F32, kind="ExternalInput")
    t["dvec"] = nc.dram_tensor("dvec", [128, 16], F32, kind="ExternalInput")
    t["identin"] = nc.dram_tensor("identin", [128, 128], F32, kind="ExternalInput")
    t["rankv"] = nc.dram_tensor("rankv", [128, 8], F32, kind="ExternalInput")
    t["out"] = nc.dram_tensor("out", [D, T], F32, kind="ExternalOutput")
    if dump is not None:
        t["dbg"] = nc.dram_tensor("dbg", [D, T], F32, kind="ExternalOutput")
    t["lnscr"] = nc.dram_tensor("lnscr", [2, 1024], F32)
    t["rowscr"] = nc.dram_tensor("rowscr", [32, 1024], F32)
    t["wescr"] = nc.dram_tensor("wescr", [32, 1024], F32)
    t["ccin"] = nc.dram_tensor("ccin", [256, 256], F32)
    t["ccout"] = nc.dram_tensor("ccout", [2048, 256], F32, addr_space="Shared")

    stack = ExitStack()
    with stack:
        def sb(name, shape, dt):
            return stack.enter_context(nc.sbuf_tensor(name, shape, dt))

        t["resid"] = sb("resid", [128, KC * T], F32)
        t["xf"] = sb("xf", [128, KC * T], BF16)
        t["arena"] = sb("arena", [128, 24576], F32)
        t["S_sb"] = sb("S_sb", [128, 256], F32)
        t["P_sb"] = sb("P_sb", [128, 256], F32)
        t["Hcur"] = sb("Hcur", [128, 256], F32)
        t["Hsel"] = sb("Hsel", [128, 256], F32)
        t["Sj"] = sb("Sj", [128, 256], F32)
        t["Pj"] = sb("Pj", [128, 256], F32)
        t["axt"] = sb("axt", [128, T], F32)
        t["A_sb"] = sb("A_sb", [128, 256], F32)
        t["D_sb"] = sb("D_sb", [128, 16], F32)
        t["ident"] = sb("ident", [128, 128], F32)
        t["gw_sb"] = sb("gw_sb", [128, 512], BF16)
        t["rank_sb"] = sb("rank_sb", [128, 8], F32)
        t["ones1"] = sb("ones1", [128, 1], F32)
        t["gones"] = sb("gones", [128, 32], F32)
        t["scol"] = sb("scol", [128, 1], F32)
        t["gcols"] = sb("gcols", [128, 16], F32)
        t["ps"] = [stack.enter_context(nc.psum_tensor(f"ps{i}", [128, 512], F32))
                   for i in range(8)]
        sems = {s: stack.enter_context(nc.semaphore(s)) for s in SEMS}
        ctx = Ctx(nc, t, sems)

        def program(block):
            emit_init(ctx, block)
            if do_ffn1 or do_ln1 or do_gate1:
                emit_barrier(ctx, block)
                emit_ln(ctx, block, 0)
            if do_ffn1 or do_gate1:
                emit_barrier(ctx, block)
                emit_gate(ctx, block, 0)
            if do_ffn1:
                emit_ffn(ctx, block, 0, n_experts=n_experts, do_shared=do_shared)
            ctx.ev["sV"][("ffn1_done",)] = ctx.cnt["sV"]
            if do_mamba:
                emit_barrier(ctx, block)
                emit_ln(ctx, block, 1)
                emit_barrier(ctx, block)
                emit_xproj(ctx, block, raw=(mamba_stage in ("xraw", "xraw4")),
                           bank_base=(4 if mamba_stage == "xraw4" else 0))
                if mamba_stage in ("scan", "exchange", "corr"):
                    emit_scan(ctx, block)
                if mamba_stage in ("exchange", "corr"):
                    emit_exchange(ctx, block)
                if mamba_stage == "corr":
                    emit_corr(ctx, block)
            if do_ffn2:
                emit_barrier(ctx, block)
                emit_ln(ctx, block, 2)
                emit_barrier(ctx, block)
                emit_gate(ctx, block, 1)
                pw = ("sV", ("cum", KC - 1)) if do_mamba else None
                pwsc = ([("sP", ("cr", KC - 1, NST - 1)),
                         ("sV", ("ct", KC - 1, NST - 1))] if do_mamba else None)
                emit_ffn(ctx, block, 1, n_experts=n_experts, do_shared=do_shared,
                         pre_wait=pw, pre_wait_sc=pwsc)
            if dump is not None:
                emit_dump(ctx, block, dump)
            emit_store(ctx, block)

        ctx.plan = True
        program(None)
        ctx.reset()
        ctx.plan = False
        with nc.Block() as block:
            program(block)
    return nc



def _prep_inputs(L0_attn_x, L0_attn_A_log, L0_attn_D, L0_attn_W_xproj,
                 L0_ffn_gate_w, L0_ffn_w1, L0_ffn_w2, L0_ffn_w3,
                 L0_ffn_sw1, L0_ffn_sw2, L0_ffn_sw3):
    f = lambda a: np.ascontiguousarray(np.asarray(a, dtype=np.float32))
    xT = np.ascontiguousarray(f(L0_attn_x).reshape(8192, 2048).T)
    w1t = np.ascontiguousarray(
        f(L0_ffn_w1).transpose(0, 2, 1).reshape(32, 16, 128, 8, 128).transpose(0, 3, 1, 2, 4))
    w3t = np.ascontiguousarray(
        f(L0_ffn_w3).transpose(0, 2, 1).reshape(32, 16, 128, 8, 128).transpose(0, 3, 1, 2, 4))
    w2t = np.ascontiguousarray(
        f(L0_ffn_w2).transpose(0, 2, 1).reshape(32, 8, 128, 16, 128).transpose(0, 3, 1, 2, 4))
    sw1t = np.ascontiguousarray(
        f(L0_ffn_sw1).T.reshape(16, 128, 16, 128).transpose(2, 0, 1, 3))
    sw3t = np.ascontiguousarray(
        f(L0_ffn_sw3).T.reshape(16, 128, 16, 128).transpose(2, 0, 1, 3))
    sw2t = np.ascontiguousarray(
        f(L0_ffn_sw2).T.reshape(16, 128, 16, 128).transpose(2, 0, 1, 3))
    wxp = np.zeros((2048, 2176), np.float32)
    wxp[:, 0:2080] = f(L0_attn_W_xproj).T
    wxt = np.ascontiguousarray(wxp.reshape(16, 128, 17, 128).transpose(2, 0, 1, 3))
    gwt = np.ascontiguousarray(f(L0_ffn_gate_w).T.reshape(16, 128, 32))
    alog = np.ascontiguousarray(
        f(L0_attn_A_log).reshape(16, 128, 16).transpose(1, 0, 2).reshape(128, 256))
    dvec = np.ascontiguousarray(f(L0_attn_D).reshape(16, 128).T)
    ident = np.eye(128, dtype=np.float32)

    shared = dict(w1t=w1t, w3t=w3t, w2t=w2t, sw1t=sw1t, sw3t=sw3t, sw2t=sw2t,
                  wxt=wxt, gwt=gwt, alog=alog, dvec=dvec, identin=ident)
    in_maps = []
    for r in range(NCORE):
        rankv = np.zeros((128, 8), np.float32)
        rankv[:, r] = 1.0
        m = dict(shared)
        m["xt"] = np.ascontiguousarray(xT[:, r * T:(r + 1) * T])
        m["rankv"] = rankv
        in_maps.append(m)
    return in_maps


def kernel(L0_attn_x, L0_attn_A_log, L0_attn_D, L0_attn_W_xproj,
           L0_ffn_gate_w, L0_ffn_w1, L0_ffn_w2, L0_ffn_w3,
           L0_ffn_sw1, L0_ffn_sw2, L0_ffn_sw3):
    in_maps = _prep_inputs(L0_attn_x, L0_attn_A_log, L0_attn_D, L0_attn_W_xproj,
                           L0_ffn_gate_w, L0_ffn_w1, L0_ffn_w2, L0_ffn_w3,
                           L0_ffn_sw1, L0_ffn_sw2, L0_ffn_sw3)
    nc = build_program()
    res = run_bass_kernel_spmd(nc, in_maps, core_ids=list(range(NCORE)))
    out = np.concatenate([res.results[i]["out"] for i in range(NCORE)], axis=1)
    return np.ascontiguousarray(out.T).reshape(1, 8192, 2048)
